# version 04 (final)
# baseline (speedup 1.0000x reference)
import numpy as np
import ml_dtypes
import concourse.bass as bass
import concourse.mybir as mybir
from concourse.bass_utils import run_bass_kernel_spmd

F32 = mybir.dt.float32
BF16 = mybir.dt.bfloat16
AF = mybir.ActivationFunctionType
ALU = mybir.AluOpType

NCORES = 8
D = 2048
T = 1024
CK = 16
SC = float(1.0 / np.sqrt(128.0))
ESH = -30.0
EPS_LN = 1e-5
EPS_RMS = 1e-6
FB = 16

ENGS = ("te", "act", "dve", "qs", "qp")
SEMS = ("PE", "ACT", "DVE", "QS", "QP", "CC")
DMA_SEMS = {"QS", "QP", "CC"}


class Emitter:
    def __init__(self):
        self.ops = {e: [] for e in ENGS}
        self.c = {s: 0 for s in SEMS}
        self.active = True

    def emit(self, eng, fn, waits=(), inc=None):
        if not self.active:
            return 0
        amt = 16 if inc in DMA_SEMS else 1
        post = None
        if inc is not None:
            self.c[inc] += amt
            post = self.c[inc]
        wd = {}
        for wv in waits:
            if wv is None:
                continue
            s, v = wv
            if v and v > wd.get(s, 0):
                wd[s] = v
        self.ops[eng].append((fn, sorted(wd.items()), inc, amt))
        return post


def build_nc(part):
    nc = bass.Bass("TRN2", target_bir_lowering=False)
    E = Emitter()
    emit = E.emit

    names = []

    def din(name, shape, dt, parts):
        if part not in parts:
            return None
        names.append(name)
        return nc.dram_tensor(name, shape, dt, kind="ExternalInput").ap()

    def dout(name, shape, dt, parts):
        if part not in parts:
            return None
        return nc.dram_tensor(name, shape, dt, kind="ExternalOutput").ap()

    xT = din("xT", [D, T], F32, (1, 2))
    wqkv = din("wqkv", [D, 3 * D], F32, (1,))
    bq = din("bq", [128, 48], F32, (1,))
    wout = din("wout", [D, D], F32, (2,))
    bo = din("bo", [128, 16], F32, (2,))
    wqT = din("wqT", [D, D], F32, (2,))
    wkT = din("wkT", [D, D], F32, (2,))
    wvT = din("wvT", [D, D], F32, (2,))
    woT = din("woT", [D, D], F32, (3,))
    amat = din("amat", [D, 16], F32, (3,))
    bmat = din("bmat", [16, 8192], F32, (3,))
    winT = din("winT", [D, 8192], F32, (3,))
    wupT = din("wupT", [D, 8192], F32, (3,))
    wdnT = din("wdnT", [8192, D], F32, (3,))
    ln1 = din("ln1", [128, 16], F32, (2,))
    ln2 = din("ln2", [128, 16], F32, (3,))
    identd = din("identd", [128, 128], BF16, (2, 3))
    onescd = din("onescd", [128, 1], BF16, (1, 2, 3))
    onesrd = din("onesrd", [1, 128], F32, (1, 2, 3))
    qgd = din("qgd", [1, T], F32, (3,))
    constv = din("constv", [128, 3], F32, (1, 2, 3))
    kcold = din("kcold", [128, 64], F32, (3,))
    qaT_in = din("qaT", [D, T], BF16, (2,))
    kvAf = din("kvAf", [NCORES * 2 * D, T], BF16, (2,))
    qbT_in = din("qbT", [D, T], BF16, (3,))
    kvBf = din("kvBf", [NCORES * 2 * D, T], BF16, (3,))
    x1i = din("x1i", [D, T], F32, (3,))

    qaT_out = dout("qaT", [D, T], BF16, (1,))
    ccA_in = dout("kvA", [2 * D, T], BF16, (1,))
    qbT_out = dout("qbT", [D, T], BF16, (2,))
    ccB_in = dout("kvB", [2 * D, T], BF16, (2,))
    x1o = dout("x1o", [D, T], F32, (2,))
    dbgA = dout("dbgA", [D, T], BF16, (2,))
    x1d = nc.dram_tensor("x1d", [D, T], F32) if part == 2 else x1i
    yT = dout("yT", [D, T], F32, (3,))

    ctx = []

    def sb(name, shape, dt):
        cm = nc.sbuf_tensor(name, shape, dt)
        t = cm.__enter__()
        ctx.append(cm)
        return t

    def ps(name, shape, dt):
        cm = nc.psum_tensor(name, shape, dt)
        t = cm.__enter__()
        ctx.append(cm)
        return t

    sbX = sb("sbX", [128, CK * T], BF16)
    sbA = sb("sbA", [128, CK * T], BF16)
    sbB = sb("sbB", [128, CK * T], BF16)
    sbKT = sb("sbKT", [128, 2048], BF16)
    sbVT = sb("sbVT", [128, 2048], BF16)
    sbVt = sb("sbVt", [128, 1024], BF16)
    sbPT = sb("sbPT", [128, 2048], BF16)
    sbW = sb("sbW", [128, 4096], BF16)
    sbW2 = sb("sbW2", [128, 4096], BF16)
    sbW3 = sb("sbW3", [128, 8192], BF16)
    sbXst = sb("sbXst", [128, 2048], F32)
    sbTmp = sb("sbTmp", [128, 2048], F32)
    sbS = sb("sbS", [128, 2048], BF16)
    sbSq = sb("sbSq", [128, 1024], BF16)
    sbRep = sb("sbRep", [128, 2048], F32)
    sbQG = sb("sbQG", [128, 1024], F32)
    sbLinv = sb("sbLinv", [128, 1024], F32)
    sbRow = sb("sbRow", [1, 4096], F32)
    sbMod = sb("sbMod", [16, 1024], BF16)
    sbMH = sb("sbMH", [16, 1024], BF16)
    sbBst = sb("sbBst", [16, 1024], BF16)
    sbAm = sb("sbAm", [128, 256], BF16)
    sbCst = sb("sbCst", [128, 1024], BF16)
    sbBq = sb("sbBq", [128, 48], F32)
    sbBo = sb("sbBo", [128, 16], F32)
    sbLn1 = sb("sbLn1", [128, 16], F32)
    sbLn2 = sb("sbLn2", [128, 16], F32)
    sbId = sb("sbId", [128, 128], BF16)
    sbOc = sb("sbOc", [128, 1], BF16)
    sbOr = sb("sbOr", [1, 128], F32)
    sbKcol = sb("sbKcol", [128, 64], F32)
    sbCv = sb("sbCv", [128, 3], F32)
    nc.const_aps.register(F32, EPS_LN, sbCv[:, 0:1])
    nc.const_aps.register(F32, EPS_RMS, sbCv[:, 1:2])
    nc.const_aps.register(F32, ESH, sbCv[:, 2:3])

    ROW_M, ROW_R, ROW_RR, ROW_M2 = 0, 1024, 2048, 3072
    ROW_R2 = ROW_RR

    pb_p = [ps("pb_p0", [128, 512], F32), ps("pb_p1", [128, 512], F32)]
    pb_s = [ps("pb_s0", [128, 512], F32), ps("pb_s1", [128, 512], F32)]
    pb_pv = [ps("pb_pv0", [128, 512], F32), ps("pb_pv1", [128, 512], F32)]
    pb_vt = ps("pb_vt", [128, 512], BF16)
    pb_rep = ps("pb_rep", [128, 512], F32)

    for dst, srct in ((sbBq, bq), (sbBo, bo), (sbLn1, ln1), (sbLn2, ln2),
                      (sbId, identd), (sbOc, onescd), (sbOr, onesrd),
                      (sbKcol, kcold), (sbCv, constv)):
        if srct is not None:
            emit("qs", lambda q, d=dst, s=srct: q.dma_start(out=d[:, :], in_=s[:, :]), inc="QS")
    if qgd is not None:
        emit("qs", lambda q: q.dma_start(out=sbQG[0:1, 0:1024], in_=qgd[:, :]), inc="QS")
    c0 = E.c["QS"]
    E.active = (part == 3)
    emit("qp", lambda q: q.dma_start(
        out=sbAm[:, :].rearrange("p (c r) -> p c r", c=16),
        in_=amat.rearrange("(c p) r -> p c r", p=128)), inc="QP")
    cAm = E.c["QP"]
    qg_done = []
    for qb in range(2):
        w = [("QS", c0)]
        if qg_done:
            w.append(("DVE", qg_done[-1]))
        v = emit("te", lambda t, qb=qb: t.matmul(
            pb_rep[:, :], sbOr[:, :], sbQG[0:1, bass.ts(qb, 512)], start=True, stop=True),
            waits=w, inc="PE")
        v2 = emit("dve", lambda d, qb=qb: d.tensor_copy(
            sbQG[:, bass.ts(qb, 512)], pb_rep[:, :]), waits=[("PE", v)], inc="DVE")
        qg_done.append(v2)
    E.active = True
    pb_rep_free = [("DVE", qg_done[1])]

    stat_prev_pe = {}

    def ln_stats_chunk(src_fn, ck, pre_waits):
        sl = ck % 2
        w = list(pre_waits)
        if 0 in stat_prev_pe:
            w.append(("PE", stat_prev_pe[0]))
        if sl in stat_prev_pe:
            w.append(("PE", stat_prev_pe[sl]))
        va = emit("act", lambda a, sl=sl: a.activation(
            sbS[:, bass.ts(sl, 1024)], src_fn(), AF.Copy), waits=w, inc="ACT")
        vq = emit("act", lambda a, sl=sl: a.activation(
            sbSq[:, 0:1024], src_fn(), AF.Square), waits=w, inc="ACT")
        last = None
        for tb in range(2):
            emit("te", lambda t, tb=tb, sl=sl, ck=ck: t.matmul(
                pb_s[tb][0:1, :], sbOc[:, :],
                sbS[:, sl * 1024 + tb * 512: sl * 1024 + tb * 512 + 512],
                start=(ck == 0), stop=(ck == CK - 1)), waits=[("ACT", va)], inc="PE")
            last = emit("te", lambda t, tb=tb, ck=ck: t.matmul(
                pb_pv[tb][0:1, :], sbOc[:, :],
                sbSq[:, tb * 512: tb * 512 + 512],
                start=(ck == 0), stop=(ck == CK - 1)), waits=[("ACT", vq)], inc="PE")
        stat_prev_pe[0] = last
        stat_prev_pe[sl] = last
        return last

    def rows_from_stats(row_m, row_r, pe_done, eps, extra=()):
        vs = []
        for tb in range(2):
            vs.append(emit("act", lambda a, tb=tb: a.activation(
                sbRow[0:1, row_m + tb * 512:row_m + tb * 512 + 512], pb_s[tb][0:1, :],
                AF.Copy, scale=1.0 / 2048.0), waits=[("PE", pe_done)], inc="ACT"))
            vs.append(emit("act", lambda a, tb=tb: a.activation(
                sbRow[0:1, row_r + tb * 512:row_r + tb * 512 + 512], pb_pv[tb][0:1, :],
                AF.Copy, scale=1.0 / 2048.0), waits=[("PE", pe_done)], inc="ACT"))
        va = max(vs)
        v1 = emit("dve", lambda d: d.scalar_tensor_tensor(
            sbTmp[0:1, 0:1024], sbRow[0:1, row_m:row_m + 1024], 1.0,
            sbRow[0:1, row_m:row_m + 1024], ALU.mult, ALU.mult),
            waits=[("ACT", va)] + list(extra), inc="DVE")
        v2 = emit("dve", lambda d: d.tensor_tensor(
            sbRow[0:1, row_r:row_r + 1024], sbRow[0:1, row_r:row_r + 1024],
            sbTmp[0:1, 0:1024], ALU.subtract), waits=[("DVE", v1)], inc="DVE")
        v2b = emit("dve", lambda d: d.tensor_copy(
            sbTmp[0:1, 1024:2048], sbRow[0:1, row_r:row_r + 1024]),
            waits=[("DVE", v2)], inc="DVE")
        v3 = emit("act", lambda a: a.activation(
            sbRow[0:1, row_r:row_r + 1024], sbRow[0:1, row_r:row_r + 1024],
            AF.Sqrt, bias=float(eps)), waits=[("DVE", v2b)], inc="ACT")
        v4 = emit("dve", lambda d: d.reciprocal(
            sbRow[0:1, row_r:row_r + 1024], sbRow[0:1, row_r:row_r + 1024]),
            waits=[("ACT", v3)], inc="DVE")
        return v4

    def replicate_rows(scale_off, bias_off, ready):
        nonlocal_pb = pb_rep_free
        outv = []
        prev = None
        for tb in range(2):
            for j, off, dst0 in ((0, scale_off, 0), (1, bias_off, 1024)):
                w = list(ready) + list(nonlocal_pb)
                if prev is not None:
                    w.append(("DVE", prev))
                v = emit("te", lambda t, off=off, tb=tb: t.matmul(
                    pb_rep[:, :], sbOr[:, :],
                    sbRow[0:1, off + tb * 512: off + tb * 512 + 512],
                    start=True, stop=True), waits=w, inc="PE")
                prev = emit("dve", lambda d, dst0=dst0, tb=tb: d.tensor_copy(
                    sbRep[:, dst0 + tb * 512: dst0 + tb * 512 + 512], pb_rep[:, :]),
                    waits=[("PE", v)], inc="DVE")
                outv.append(prev)
        nonlocal_pb.clear()
        nonlocal_pb.append(("DVE", max(outv)))
        return max(outv)

    proj_state = {"slot": {}, "bank": {}}

    def projection(wsrc, n_os, rhs_sb, rhs_ready, evac_fn, wstage):
        for os in range(n_os):
            sl = os % 2
            w = []
            if sl in proj_state["slot"]:
                w.append(("PE", proj_state["slot"][sl]))
            dv = None
            for half in range(2):
                dv = emit("qp", lambda q, os=os, sl=sl, half=half, wsrc=wsrc, wstage=wstage:
                          q.dma_start(
                              out=wstage[:, sl * 2048 + half * 1024:
                                         sl * 2048 + half * 1024 + 1024].rearrange(
                                             "p (c o) -> p c o", c=8),
                              in_=wsrc[half * 1024:(half + 1) * 1024,
                                       os * 128:(os + 1) * 128].rearrange(
                                           "(c p) o -> p c o", p=128)),
                          waits=w if half == 0 else [], inc="QP")
            for tb in range(2):
                par = (os * 2 + tb) % 2
                mw = [("QP", dv), rhs_ready]
                if par in proj_state["bank"]:
                    mw.append(proj_state["bank"][par])
                pe_done = None
                for ck in range(CK):
                    pe_done = emit("te", lambda t, sl=sl, tb=tb, ck=ck, par=par,
                                   rhs_sb=rhs_sb, wstage=wstage: t.matmul(
                                       pb_p[par][:, :],
                                       wstage[:, sl * 2048 + ck * 128: sl * 2048 + ck * 128 + 128],
                                       rhs_sb[:, ck * 1024 + tb * 512: ck * 1024 + tb * 512 + 512],
                                       start=(ck == 0), stop=(ck == CK - 1)),
                                   waits=mw if ck == 0 else [], inc="PE")
                proj_state["slot"][sl] = pe_done
                proj_state["bank"][par] = evac_fn(os, tb, par, pe_done)

    E.active = (part == 1)
    xact = {}
    pe_last = None
    for ck in range(CK):
        sl = ck % 2
        w = [("QS", c0)]
        if ck >= 2:
            w.append(("ACT", xact[ck - 2]))
        v = emit("qs", lambda q, ck=ck, sl=sl: q.dma_start(
            out=sbXst[:, bass.ts(sl, 1024)], in_=xT[bass.ts(ck, 128), :]), waits=w, inc="QS")
        pe_last = ln_stats_chunk(lambda sl=sl: sbXst[:, bass.ts(sl, 1024)], ck, [("QS", v)])
        xact[ck] = E.c["ACT"]
    vr = rows_from_stats(ROW_M, ROW_R, pe_last, EPS_LN)
    rep_done = replicate_rows(ROW_R, ROW_M, [("DVE", vr)])

    ain_done = {}
    for ck in range(CK):
        sl = ck % 2
        w = [("DVE", rep_done)]
        if ck >= 2:
            w.append(("DVE", ain_done[ck - 2]))
        v = emit("qs", lambda q, ck=ck, sl=sl: q.dma_start(
            out=sbXst[:, bass.ts(sl, 1024)], in_=xT[bass.ts(ck, 128), :]), waits=w, inc="QS")
        v1 = emit("dve", lambda d, sl=sl: d.tensor_tensor(
            sbTmp[:, bass.ts(sl, 1024)], sbXst[:, bass.ts(sl, 1024)],
            sbRep[:, 1024:2048], ALU.subtract), waits=[("QS", v)], inc="DVE")
        v2 = emit("dve", lambda d, ck=ck, sl=sl: d.tensor_tensor(
            sbA[:, bass.ts(ck, 1024)], sbTmp[:, bass.ts(sl, 1024)],
            sbRep[:, 0:1024], ALU.mult), waits=[("DVE", v1)], inc="DVE")
        ain_done[ck] = v2
    ain_all = max(ain_done.values())

    contrib_sends = []
    cst_free = {}

    def qkv_evac_factory(cc_in, with_bias, bias_base, qdst):
        def evac(os, tb, par, pe_done):
            if os < 16:
                if with_bias:
                    v = emit("dve", lambda d, os=os, tb=tb, par=par: d.tensor_scalar(
                        sbB[:, os * 1024 + tb * 512: os * 1024 + tb * 512 + 512],
                        pb_p[par][:, :], sbBq[:, bias_base + os: bias_base + os + 1],
                        None, ALU.add), waits=[("PE", pe_done)], inc="DVE")
                    sem = "DVE"
                else:
                    v = emit("act", lambda a, os=os, tb=tb, par=par: a.activation(
                        sbB[:, os * 1024 + tb * 512: os * 1024 + tb * 512 + 512],
                        pb_p[par][:, :], AF.Copy), waits=[("PE", pe_done)], inc="ACT")
                    sem = "ACT"
                if tb == 1:
                    emit("qs", lambda q, os=os, qdst=qdst: q.dma_start(
                        out=qdst[os * 128:(os + 1) * 128, :],
                        in_=sbB[:, bass.ts(os, 1024)]), waits=[(sem, v)], inc="QS")
                return (sem, v)
            kv = os - 16
            w = [("PE", pe_done)]
            if 0 in cst_free:
                w.append(("QS", cst_free[0]))
            if with_bias:
                v = emit("dve", lambda d, os=os, tb=tb, par=par: d.tensor_scalar(
                    sbCst[:, tb * 512: tb * 512 + 512],
                    pb_p[par][:, :], sbBq[:, bias_base + os: bias_base + os + 1],
                    None, ALU.add), waits=w, inc="DVE")
                sem = "DVE"
            else:
                v = emit("act", lambda a, os=os, tb=tb, par=par: a.activation(
                    sbCst[:, tb * 512: tb * 512 + 512],
                    pb_p[par][:, :], AF.Copy), waits=w, inc="ACT")
                sem = "ACT"
            if tb == 1:
                dv = emit("qs", lambda q, kv=kv, cc_in=cc_in: q.dma_start(
                    out=cc_in[kv * 128:(kv + 1) * 128, :], in_=sbCst[:, 0:1024]),
                    waits=[(sem, v)], inc="QS")
                cst_free[0] = dv
                contrib_sends.append(dv)
            return (sem, v)
        return evac

    evA = qkv_evac_factory(ccA_in, True, 0, qaT_out)

    class WQKVCols:
        def __init__(self, base):
            self.base = base

        def __getitem__(self, idx):
            rows, cols = idx
            return wqkv[rows, self.base + cols.start: self.base + cols.stop]

    projection(WQKVCols(0), 16, sbA, ("DVE", ain_all),
               lambda os, tb, par, pe: evA(os, tb, par, pe), sbW)
    projection(WQKVCols(16 * 128), 16, sbA, ("DVE", ain_all),
               lambda os, tb, par, pe: evA(os + 16, tb, par, pe), sbW)
    projection(WQKVCols(32 * 128), 16, sbA, ("DVE", ain_all),
               lambda os, tb, par, pe: evA(os + 32, tb, par, pe), sbW)

    def attention(QT_sb, out_sb, cc_out, cc_ready, causal):
        if causal:
            need = {0: [kt for kt in range(64) if kt % 8 <= 3], 1: list(range(64))}
            masked = {0: set(need[0]), 1: set(kt for kt in range(64) if kt % 8 >= 3)}
        else:
            need = {0: list(range(64)), 1: list(range(64))}
            masked = {0: set(), 1: set()}
        last_kt = {qb: need[qb][-1] for qb in (0, 1)}
        prev_norm_done = None
        kv_slot_free = {}
        vt_tile_free = {}
        pvt_bank_free = {}
        pt_free = {}
        exp_act = {}
        kv_dma = {}
        def emit_kv_dma(h, r):
            for which, src_off, dstb in (("K", 0, sbKT), ("V", D, sbVT)):
                sl = r % 2
                w = [("CC", cc_ready)]
                if (which, sl) in kv_slot_free:
                    w.append(("PE", kv_slot_free[(which, sl)]))
                kv_dma[(which, h, r)] = emit(
                    "qs", lambda q, h=h, r=r, so=src_off, dstb=dstb, sl=sl, cc_out=cc_out:
                    q.dma_start(out=dstb[:, bass.ts(sl, 1024)],
                                in_=cc_out[r * 2 * D + so + h * 128:
                                           r * 2 * D + so + h * 128 + 128, :]),
                    waits=w, inc="QS")

        for h in range(16):
            emit_kv_dma(h, 0)
            emit_kv_dma(h, 1)
            nth = {0: 0, 1: 0}
            for r in range(8):
                for klocal in range(8):
                    kt = r * 8 + klocal
                    sl = r % 2
                    vtslot = kt % 4
                    w = [("QS", kv_dma[("V", h, r)])]
                    if vtslot in pvt_bank_free:
                        w.append(("DVE", pvt_bank_free[vtslot]))
                    vtr = emit("te", lambda t, sl=sl, klocal=klocal, vtslot=vtslot: t.transpose(
                        pb_vt[:, bass.ts(vtslot, 128)],
                        sbVT[:, sl * 1024 + klocal * 128: sl * 1024 + klocal * 128 + 128],
                        sbId[:, :]), waits=w, inc="PE")
                    kv_slot_free[("V", sl)] = vtr
                    w = [("PE", vtr)]
                    if kt % 8 in vt_tile_free:
                        w.append(("PE", vt_tile_free[kt % 8]))
                    vev = emit("dve", lambda d, kt=kt, vtslot=vtslot: d.tensor_copy(
                        sbVt[:, bass.ts(kt % 8, 128)], pb_vt[:, bass.ts(vtslot, 128)]),
                        waits=w, inc="DVE")
                    pvt_bank_free[vtslot] = vev
                    for qb in (0, 1):
                        if kt not in need[qb]:
                            continue
                        ptsl = qb * 2 + nth[qb] % 2
                        nth[qb] += 1
                        w = [("QS", kv_dma[("K", h, r)])]
                        if (qb, "prev") in exp_act:
                            w.append(("ACT", exp_act[(qb, "prev")]))
                        smm = emit("te", lambda t, qb=qb, sl=sl, klocal=klocal, h=h,
                                   QT_sb=QT_sb: t.matmul(
                                       pb_s[qb][:, :],
                                       sbKT[:, sl * 1024 + klocal * 128: sl * 1024 + klocal * 128 + 128],
                                       QT_sb[:, h * 1024 + qb * 512: h * 1024 + qb * 512 + 512],
                                       start=True, stop=True), waits=w, inc="PE")
                        kv_slot_free[("K", sl)] = smm
                        w = [("PE", smm)]
                        if ptsl in pt_free:
                            w.append(pt_free[ptsl])
                        ev = emit("act", lambda a, qb=qb, ptsl=ptsl: a.activation(
                            sbPT[:, bass.ts(ptsl, 512)], pb_s[qb][:, :], AF.Exp,
                            bias=ESH, scale=SC), waits=w, inc="ACT")
                        exp_act[(qb, "prev")] = ev
                        first = (kt == need[qb][0])
                        lastk = (kt == last_kt[qb])
                        if causal and kt in masked[qb]:
                            w = [("ACT", ev), ("DVE", qg_done[qb])]
                            mv = emit("dve", lambda d, qb=qb, kt=kt, ptsl=ptsl:
                                      d.scalar_tensor_tensor(
                                          sbPT[:, bass.ts(ptsl, 512)],
                                          sbQG[:, bass.ts(qb, 512)],
                                          sbKcol[:, kt:kt + 1],
                                          sbPT[:, bass.ts(ptsl, 512)],
                                          ALU.is_ge, ALU.mult), waits=w, inc="DVE")
                            psrc = sbPT
                            pw = [("DVE", mv)]
                            is_masked = True
                        else:
                            psrc = sbPT
                            pw = [("ACT", ev)]
                            is_masked = False
                        if first and prev_norm_done:
                            pw.append(("DVE", prev_norm_done))
                        emit("te", lambda t, qb=qb, ptsl=ptsl, psrc=psrc, first=first,
                             lastk=lastk: t.matmul(
                                 pb_p[qb][0:1, :], sbOc[:, :], psrc[:, bass.ts(ptsl, 512)],
                                 start=first, stop=lastk), waits=pw, inc="PE")
                        pvv = emit("te", lambda t, qb=qb, ptsl=ptsl, psrc=psrc, kt=kt,
                                   first=first, lastk=lastk: t.matmul(
                                       pb_pv[qb][:, :], sbVt[:, bass.ts(kt % 8, 128)],
                                       psrc[:, bass.ts(ptsl, 512)],
                                       start=first, stop=lastk),
                                   waits=[("DVE", vev)], inc="PE")
                        pt_free[ptsl] = ("PE", pvv)
                        vt_tile_free[kt % 8] = pvv
                if r + 2 <= 7:
                    emit_kv_dma(h, r + 2)
            norm_vals = []
            for qb in (0, 1):
                pe_all = E.c["PE"]
                v1 = emit("dve", lambda d, qb=qb: d.reciprocal(
                    sbRow[0:1, ROW_M2 + qb * 512: ROW_M2 + qb * 512 + 512],
                    pb_p[qb][0:1, :]), waits=[("PE", pe_all)], inc="DVE")
                v2 = emit("te", lambda t, qb=qb: t.matmul(
                    pb_rep[:, :], sbOr[:, :],
                    sbRow[0:1, ROW_M2 + qb * 512: ROW_M2 + qb * 512 + 512],
                    start=True, stop=True), waits=[("DVE", v1)], inc="PE")
                v3 = emit("dve", lambda d, qb=qb: d.tensor_copy(
                    sbLinv[:, bass.ts(qb, 512)], pb_rep[:, :]), waits=[("PE", v2)], inc="DVE")
                v4 = emit("dve", lambda d, qb=qb, h=h, out_sb=out_sb: d.tensor_tensor(
                    out_sb[:, h * 1024 + qb * 512: h * 1024 + qb * 512 + 512],
                    pb_pv[qb][:, :], sbLinv[:, bass.ts(qb, 512)], ALU.mult),
                    waits=[("DVE", v3)], inc="DVE")
                norm_vals.append(v4)
            prev_norm_done = max(norm_vals)
        pb_rep_free.clear()
        pb_rep_free.append(("DVE", prev_norm_done))
        return prev_norm_done

    E.active = (part == 2)
    for ck in range(CK):
        emit("qs", lambda q, ck=ck: q.dma_start(
            out=sbB[:, bass.ts(ck, 1024)], in_=qaT_in[bass.ts(ck, 128), :]),
            waits=[("QS", c0)], inc="QS")
    attnA_done = attention(sbB, sbA, kvAf, 0, causal=False)
    for ck in range(CK):
        emit("qs", lambda q, ck=ck: q.dma_start(
            out=dbgA[bass.ts(ck, 128), :], in_=sbA[:, bass.ts(ck, 1024)]),
            waits=[("DVE", attnA_done)], inc="QS")

    stats_pe = [None]
    sq_prev = {}

    def outproj_evac(os, tb, par, pe_done):
        v = emit("dve", lambda d, os=os, tb=tb, par=par: d.tensor_scalar(
            sbB[:, os * 1024 + tb * 512: os * 1024 + tb * 512 + 512],
            pb_p[par][:, :], sbBo[:, os:os + 1], None, ALU.add),
            waits=[("PE", pe_done)], inc="DVE")
        w = [("DVE", v)]
        if 0 in sq_prev:
            w.append(("PE", sq_prev[0]))
        vq = emit("act", lambda a, os=os, tb=tb: a.activation(
            sbSq[:, 0:512],
            sbB[:, os * 1024 + tb * 512: os * 1024 + tb * 512 + 512], AF.Square),
            waits=w, inc="ACT")
        first = (os == 0)
        lasto = (os == 15)
        emit("te", lambda t, os=os, tb=tb, first=first, lasto=lasto: t.matmul(
            pb_s[tb][0:1, :], sbOc[:, :],
            sbB[:, os * 1024 + tb * 512: os * 1024 + tb * 512 + 512],
            start=first, stop=lasto), waits=[("DVE", v)], inc="PE")
        sq_prev[0] = stats_pe[0] = emit(
            "te", lambda t, tb=tb, first=first, lasto=lasto: t.matmul(
                pb_pv[tb][0:1, :], sbOc[:, :], sbSq[:, 0:512],
                start=first, stop=lasto), waits=[("ACT", vq)], inc="PE")
        return ("DVE", v)

    projection(wout, 16, sbA, ("DVE", attnA_done), outproj_evac, sbW)
    vr2 = rows_from_stats(ROW_M2, ROW_R2, stats_pe[0], EPS_LN)
    rep2 = replicate_rows(ROW_R2, ROW_M2, [("DVE", vr2)])

    E.active = (part == 2)
    x1_dve = {}
    x1_dma = {}
    stat_prev_pe.clear()
    for ck in range(CK):
        sl = ck % 2
        w = [("DVE", rep2)]
        if ck >= 2:
            w.append(("DVE", x1_dve[ck - 2]))
        v = emit("qs", lambda q, ck=ck, sl=sl: q.dma_start(
            out=sbXst[:, bass.ts(sl, 1024)], in_=xT[bass.ts(ck, 128), :]), waits=w, inc="QS")
        w2 = [("QS", v)]
        if ck >= 2:
            w2.append(("QS", x1_dma[ck - 2]))
        v1 = emit("dve", lambda d, ck=ck, sl=sl: d.tensor_tensor(
            sbTmp[:, bass.ts(sl, 1024)], sbB[:, bass.ts(ck, 1024)],
            sbRep[:, 1024:2048], ALU.subtract), waits=w2, inc="DVE")
        v2 = emit("dve", lambda d, sl=sl: d.tensor_tensor(
            sbTmp[:, bass.ts(sl, 1024)], sbTmp[:, bass.ts(sl, 1024)],
            sbRep[:, 0:1024], ALU.mult), waits=[("DVE", v1)], inc="DVE")
        v3 = emit("dve", lambda d, sl=sl: d.tensor_tensor(
            sbTmp[:, bass.ts(sl, 1024)], sbTmp[:, bass.ts(sl, 1024)],
            sbXst[:, bass.ts(sl, 1024)], ALU.add), waits=[("DVE", v2)], inc="DVE")
        x1_dve[ck] = v3
        emit("qs", lambda q, ck=ck, sl=sl: q.dma_start(
            out=x1o[bass.ts(ck, 128), :], in_=sbTmp[:, bass.ts(sl, 1024)]),
            waits=[("DVE", v3)], inc="QS")
        x1_dma[ck] = emit("qs", lambda q, ck=ck, sl=sl: q.dma_start(
            out=x1d[bass.ts(ck, 128), :], in_=sbTmp[:, bass.ts(sl, 1024)]),
            waits=[("DVE", v3)], inc="QS")
        pe_last = ln_stats_chunk(lambda sl=sl: sbTmp[:, bass.ts(sl, 1024)], ck, [("DVE", v3)])
    E.active = (part == 3)
    stat_prev_pe.clear()
    xact3 = {}
    for ck in range(CK):
        sl = ck % 2
        w = [("QS", c0)]
        if ck >= 2:
            w.append(("ACT", xact3[ck - 2]))
        v = emit("qs", lambda q, ck=ck, sl=sl: q.dma_start(
            out=sbXst[:, bass.ts(sl, 1024)], in_=x1d[bass.ts(ck, 128), :]), waits=w, inc="QS")
        p3 = ln_stats_chunk(lambda sl=sl: sbXst[:, bass.ts(sl, 1024)], ck, [("QS", v)])
        xact3[ck] = E.c["ACT"]
    if part == 3:
        pe_last = p3
    E.active = (part in (2, 3))
    vrB = rows_from_stats(ROW_M, ROW_R, pe_last, EPS_LN,
                          extra=[("QS", x1_dma.get(CK - 1, 0))])
    v1 = emit("dve", lambda d: d.scalar_tensor_tensor(
        sbRow[0:1, ROW_RR:ROW_RR + 1024], sbRow[0:1, ROW_R:ROW_R + 1024], 1.0,
        sbRow[0:1, ROW_R:ROW_R + 1024], ALU.mult, ALU.mult), waits=[("DVE", vrB)], inc="DVE")
    v2 = emit("dve", lambda d: d.tensor_tensor(
        sbRow[0:1, ROW_RR:ROW_RR + 1024], sbRow[0:1, ROW_RR:ROW_RR + 1024],
        sbTmp[0:1, 1024:2048], ALU.mult), waits=[("DVE", v1)], inc="DVE")
    v3 = emit("act", lambda a: a.activation(
        sbRow[0:1, ROW_RR:ROW_RR + 1024], sbRow[0:1, ROW_RR:ROW_RR + 1024],
        AF.Sqrt, bias=EPS_RMS), waits=[("DVE", v2)], inc="ACT")
    v4 = emit("dve", lambda d: d.reciprocal(
        sbRow[0:1, ROW_RR:ROW_RR + 1024], sbRow[0:1, ROW_RR:ROW_RR + 1024]),
        waits=[("ACT", v3)], inc="DVE")
    vrr = emit("dve", lambda d: d.tensor_tensor(
        sbRow[0:1, ROW_RR:ROW_RR + 1024], sbRow[0:1, ROW_RR:ROW_RR + 1024],
        sbRow[0:1, ROW_R:ROW_R + 1024], ALU.mult), waits=[("DVE", v4)], inc="DVE")

    def rms_apply(dst_sb, wcol_sb, waits0):
        done = {}
        for ck in range(CK):
            sl = ck % 2
            w = list(waits0) + [("QS", x1_dma.get(CK - 1, 0))]
            if ck >= 2:
                w.append(("DVE", done[ck - 2]))
            v = emit("qs", lambda q, ck=ck, sl=sl: q.dma_start(
                out=sbXst[:, bass.ts(sl, 1024)], in_=x1d[bass.ts(ck, 128), :]),
                waits=w, inc="QS")
            v1 = emit("dve", lambda d, sl=sl: d.tensor_tensor(
                sbTmp[:, bass.ts(sl, 1024)], sbXst[:, bass.ts(sl, 1024)],
                sbRep[:, 1024:2048], ALU.subtract), waits=[("QS", v)], inc="DVE")
            v2 = emit("dve", lambda d, sl=sl: d.tensor_tensor(
                sbTmp[:, bass.ts(sl, 1024)], sbTmp[:, bass.ts(sl, 1024)],
                sbRep[:, 0:1024], ALU.mult), waits=[("DVE", v1)], inc="DVE")
            v3 = emit("dve", lambda d, ck=ck, sl=sl, dst_sb=dst_sb, wc=wcol_sb: d.tensor_scalar(
                dst_sb[:, bass.ts(ck, 1024)], sbTmp[:, bass.ts(sl, 1024)],
                wc[:, ck:ck + 1], None, ALU.mult), waits=[("DVE", v2)], inc="DVE")
            done[ck] = v3
        return max(done.values())

    E.active = (part == 2)
    repB = replicate_rows(ROW_RR, ROW_M, [("DVE", vrr)])
    xr_done = rms_apply(sbA, sbLn1, [("DVE", repB)])

    contrib_sends.clear()
    cst_free.clear()
    evB = qkv_evac_factory(ccB_in, False, 0, qbT_out)
    projection(wqT, 16, sbA, ("DVE", xr_done),
               lambda os, tb, par, pe: evB(os, tb, par, pe), sbW)
    projection(wkT, 16, sbA, ("DVE", xr_done),
               lambda os, tb, par, pe: evB(os + 16, tb, par, pe), sbW)
    projection(wvT, 16, sbA, ("DVE", xr_done),
               lambda os, tb, par, pe: evB(os + 32, tb, par, pe), sbW)
    E.active = (part == 3)
    for ck in range(CK):
        emit("qs", lambda q, ck=ck: q.dma_start(
            out=sbB[:, bass.ts(ck, 1024)], in_=qbT_in[bass.ts(ck, 128), :]),
            waits=[("QS", c0)], inc="QS")
    attnB_done = attention(sbB, sbA, kvBf, 0, causal=True)

    cst_free.clear()
    modpre_pe = [None]
    wo_prev = {}

    def wo_evac(os, tb, par, pe_done):
        w = [("PE", pe_done)]
        if 0 in wo_prev:
            w.append(("PE", wo_prev[0]))
        v = emit("act", lambda a, par=par, tb=tb: a.activation(
            sbCst[:, tb * 512: tb * 512 + 512],
            pb_p[par][:, :], AF.Copy), waits=w, inc="ACT")
        mp = emit("te", lambda t, os=os, tb=tb: t.matmul(
            pb_s[tb][0:16, :], sbAm[:, os * 16:(os + 1) * 16],
            sbCst[:, tb * 512: tb * 512 + 512],
            start=(os == 0), stop=(os == 15)), waits=[("ACT", v), ("QP", cAm)], inc="PE")
        wo_prev[0] = mp
        modpre_pe[0] = mp
        return ("ACT", v)

    projection(woT, 16, sbA, ("DVE", attnB_done), wo_evac, sbW)
    pe_after_wo = E.c["PE"]
    tanh_done = None
    for tb in range(2):
        tanh_done = emit("act", lambda a, tb=tb: a.activation(
            sbMod[:, bass.ts(tb, 512)], pb_s[tb][0:16, :], AF.Tanh),
            waits=[("PE", modpre_pe[0])], inc="ACT")

    repB2 = replicate_rows(ROW_RR, ROW_M, [("DVE", vrr), ("PE", pe_after_wo)])
    h_done = rms_apply(sbA, sbLn2, [("DVE", repB2), ("PE", pe_after_wo)])

    ha_pe = None
    for tb in range(2):
        for ck in range(CK):
            ha_pe = emit("te", lambda t, tb=tb, ck=ck: t.matmul(
                pb_s[tb][0:16, :], sbAm[:, ck * 16:(ck + 1) * 16],
                sbA[:, ck * 1024 + tb * 512: ck * 1024 + tb * 512 + 512],
                start=(ck == 0), stop=(ck == CK - 1)),
                waits=[("DVE", h_done), ("ACT", tanh_done)] if ck == 0 else [], inc="PE")
    mh_all = None
    for tb in range(2):
        mh_all = emit("dve", lambda d, tb=tb: d.scalar_tensor_tensor(
            sbMH[:, bass.ts(tb, 512)], sbMod[:, bass.ts(tb, 512)], 0.1,
            pb_s[tb][0:16, :], ALU.mult, ALU.mult), waits=[("PE", ha_pe)], inc="DVE")

    bst_pe = {}
    w3_free = None
    gu_free = None
    sbS_free = {}
    par_ctr = [0]
    bank = proj_state["bank"]
    wslot = {}
    w2slot = {}
    mlp_done = None
    for fb in range(FB):
        sl = fb % 2
        w = []
        if sl in bst_pe:
            w.append(("PE", bst_pe[sl]))
        bdv = emit("qp", lambda q, fb=fb, sl=sl: q.dma_start(
            out=sbBst[:, bass.ts(sl, 512)], in_=bmat[:, bass.ts(fb, 512)]),
            waits=w, inc="QP")
        wdn_done = None
        for fc in range(4):
            w = []
            if w3_free is not None:
                w.append(("PE", w3_free))
            wdn_done = emit("qp", lambda q, fb=fb, fc=fc: q.dma_start(
                out=sbW3[:, bass.ts(fc, 2048)],
                in_=wdnT[fb * 512 + fc * 128: fb * 512 + (fc + 1) * 128, :]),
                waits=w, inc="QP")
        last_b_pe = None
        gu_vals = []
        for fc in range(4):
            fsg = fb * 4 + fc
            wsl = fsg % 2
            w = []
            if wsl in wslot:
                w.append(("PE", wslot[wsl]))
            widv = None
            for half in range(2):
                widv = emit("qp", lambda q, fsg=fsg, wsl=wsl, half=half: q.dma_start(
                    out=sbW[:, wsl * 2048 + half * 1024:
                            wsl * 2048 + half * 1024 + 1024].rearrange("p (c o) -> p c o", c=8),
                    in_=winT[half * 1024:(half + 1) * 1024,
                             fsg * 128:(fsg + 1) * 128].rearrange("(c p) o -> p c o", p=128)),
                    waits=w if half == 0 else [], inc="QP")
            w = []
            if wsl in w2slot:
                w.append(("PE", w2slot[wsl]))
            wudv = None
            for half in range(2):
                wudv = emit("qp", lambda q, fsg=fsg, wsl=wsl, half=half: q.dma_start(
                    out=sbW2[:, wsl * 2048 + half * 1024:
                             wsl * 2048 + half * 1024 + 1024].rearrange("p (c o) -> p c o", c=8),
                    in_=wupT[half * 1024:(half + 1) * 1024,
                             fsg * 128:(fsg + 1) * 128].rearrange("(c p) o -> p c o", p=128)),
                    waits=w if half == 0 else [], inc="QP")
            for tb in range(2):
                par = par_ctr[0] % 2
                par_ctr[0] += 1
                w = [("QP", widv), ("QP", bdv), ("DVE", mh_all), ("DVE", h_done)]
                if par in bank:
                    w.append(bank[par])
                gpe = None
                for ck in range(CK):
                    gpe = emit("te", lambda t, wsl=wsl, ck=ck, tb=tb, par=par: t.matmul(
                        pb_p[par][:, :],
                        sbW[:, wsl * 2048 + ck * 128: wsl * 2048 + ck * 128 + 128],
                        sbA[:, ck * 1024 + tb * 512: ck * 1024 + tb * 512 + 512],
                        start=(ck == 0), stop=False), waits=w if ck == 0 else [], inc="PE")
                gpe = emit("te", lambda t, fc=fc, sl=sl, tb=tb, par=par: t.matmul(
                    pb_p[par][:, :], sbBst[:, sl * 512 + fc * 128: sl * 512 + fc * 128 + 128],
                    sbMH[:, bass.ts(tb, 512)], start=False, stop=True), inc="PE")
                wslot[wsl] = gpe
                last_b_pe = gpe
                w = [("PE", gpe)]
                if (fc % 2, tb) in sbS_free:
                    w.append(("DVE", sbS_free[(fc % 2, tb)]))
                gv = emit("act", lambda a, fc=fc, tb=tb, par=par: a.activation(
                    sbS[:, fc % 2 * 1024 + tb * 512: fc % 2 * 1024 + tb * 512 + 512],
                    pb_p[par][:, :], AF.Silu), waits=w, inc="ACT")
                bank[par] = ("ACT", gv)
                par2 = par_ctr[0] % 2
                par_ctr[0] += 1
                w = [("QP", wudv)]
                if par2 in bank:
                    w.append(bank[par2])
                upe = None
                for ck in range(CK):
                    upe = emit("te", lambda t, wsl=wsl, ck=ck, tb=tb, par2=par2: t.matmul(
                        pb_p[par2][:, :],
                        sbW2[:, wsl * 2048 + ck * 128: wsl * 2048 + ck * 128 + 128],
                        sbA[:, ck * 1024 + tb * 512: ck * 1024 + tb * 512 + 512],
                        start=(ck == 0), stop=(ck == CK - 1)), waits=w if ck == 0 else [],
                        inc="PE")
                w2slot[wsl] = upe
                w = [("ACT", gv), ("PE", upe)]
                if gu_free is not None:
                    w.append(("PE", gu_free))
                guv = emit("dve", lambda d, fc=fc, tb=tb, par2=par2: d.tensor_tensor(
                    sbB[:, fc * 1024 + tb * 512: fc * 1024 + tb * 512 + 512],
                    sbS[:, fc % 2 * 1024 + tb * 512: fc % 2 * 1024 + tb * 512 + 512],
                    pb_p[par2][:, :], ALU.mult), waits=w, inc="DVE")
                sbS_free[(fc % 2, tb)] = guv
                bank[par2] = ("DVE", guv)
                gu_vals.append(guv)
        bst_pe[sl] = last_b_pe
        gu_all = max(gu_vals)
        dn_last = None
        for os in range(CK):
            for tb in range(2):
                par = par_ctr[0] % 2
                par_ctr[0] += 1
                w = [("QP", wdn_done), ("DVE", gu_all)]
                if par in bank:
                    w.append(bank[par])
                dpe = None
                for fc in range(4):
                    dpe = emit("te", lambda t, fc=fc, os=os, tb=tb, par=par: t.matmul(
                        pb_p[par][:, :],
                        sbW3[:, fc * 2048 + os * 128: fc * 2048 + os * 128 + 128],
                        sbB[:, fc * 1024 + tb * 512: fc * 1024 + tb * 512 + 512],
                        start=(fc == 0), stop=(fc == 3)), waits=w if fc == 0 else [], inc="PE")
                dn_last = dpe
                if fb == 0:
                    av = emit("dve", lambda d, os=os, tb=tb, par=par: d.tensor_copy(
                        sbX[:, os * 1024 + tb * 512: os * 1024 + tb * 512 + 512],
                        pb_p[par][:, :]), waits=[("PE", dpe)], inc="DVE")
                else:
                    av = emit("dve", lambda d, os=os, tb=tb, par=par: d.tensor_tensor(
                        sbX[:, os * 1024 + tb * 512: os * 1024 + tb * 512 + 512],
                        sbX[:, os * 1024 + tb * 512: os * 1024 + tb * 512 + 512],
                        pb_p[par][:, :], ALU.add), waits=[("PE", dpe)], inc="DVE")
                bank[par] = ("DVE", av)
                mlp_done = av
        w3_free = dn_last
        gu_free = dn_last

    repF = replicate_rows(ROW_R, ROW_M, [("DVE", vrr), ("DVE", mlp_done)])
    stat_prev_pe.clear()
    fout_act = {}
    for ck in range(CK):
        sl = ck % 2
        w = [("DVE", repF), ("DVE", mlp_done)]
        if ck >= 2:
            w.append(("ACT", fout_act[ck - 2]))
        v = emit("qs", lambda q, ck=ck, sl=sl: q.dma_start(
            out=sbXst[:, bass.ts(sl, 1024)], in_=x1d[bass.ts(ck, 128), :]), waits=w, inc="QS")
        v1 = emit("dve", lambda d, sl=sl: d.tensor_tensor(
            sbTmp[:, bass.ts(sl, 1024)], sbXst[:, bass.ts(sl, 1024)],
            sbRep[:, 1024:2048], ALU.subtract), waits=[("QS", v)], inc="DVE")
        v2 = emit("dve", lambda d, sl=sl: d.tensor_tensor(
            sbTmp[:, bass.ts(sl, 1024)], sbTmp[:, bass.ts(sl, 1024)],
            sbRep[:, 0:1024], ALU.mult), waits=[("DVE", v1)], inc="DVE")
        v3 = emit("dve", lambda d, ck=ck, sl=sl: d.tensor_tensor(
            sbX[:, bass.ts(ck, 1024)], sbX[:, bass.ts(ck, 1024)],
            sbTmp[:, bass.ts(sl, 1024)], ALU.add), waits=[("DVE", v2)], inc="DVE")
        pe_last = ln_stats_chunk(lambda ck=ck: sbX[:, bass.ts(ck, 1024)], ck, [("DVE", v3)])
        fout_act[ck] = E.c["ACT"]
    vrF = rows_from_stats(ROW_M2, ROW_R2, pe_last, EPS_LN)
    repY = replicate_rows(ROW_R2, ROW_M2, [("DVE", vrF)])

    y_dma = {}
    for ck in range(CK):
        sl = ck % 2
        w = [("DVE", repY)]
        if ck >= 2:
            w.append(("QS", y_dma[ck - 2]))
        v = emit("qs", lambda q, ck=ck, sl=sl: q.dma_start(
            out=sbXst[:, bass.ts(sl, 1024)], in_=x1d[bass.ts(ck, 128), :]), waits=w, inc="QS")
        v1 = emit("dve", lambda d, ck=ck, sl=sl: d.tensor_tensor(
            sbTmp[:, bass.ts(sl, 1024)], sbX[:, bass.ts(ck, 1024)],
            sbRep[:, 1024:2048], ALU.subtract), waits=[("QS", v)], inc="DVE")
        v2 = emit("dve", lambda d, sl=sl: d.tensor_tensor(
            sbTmp[:, bass.ts(sl, 1024)], sbTmp[:, bass.ts(sl, 1024)],
            sbRep[:, 0:1024], ALU.mult), waits=[("DVE", v1)], inc="DVE")
        v3 = emit("dve", lambda d, sl=sl: d.tensor_tensor(
            sbTmp[:, bass.ts(sl, 1024)], sbTmp[:, bass.ts(sl, 1024)],
            sbXst[:, bass.ts(sl, 1024)], ALU.add), waits=[("DVE", v2)], inc="DVE")
        y_dma[ck] = emit("qs", lambda q, ck=ck, sl=sl: q.dma_start(
            out=yT[bass.ts(ck, 128), :], in_=sbTmp[:, bass.ts(sl, 1024)]),
            waits=[("DVE", v3)], inc="QS")
    final_qs = E.c["QS"]

    with nc.Block() as block:
        sem_cms = []
        sem_handles = {}
        for s in SEMS:
            cm = nc.semaphore(s)
            sem_handles[s] = cm.__enter__()
            sem_cms.append(cm)

        def runner(handle, key, final=None):
            for fn, waits, inc, amt in E.ops[key]:
                for (s, v) in waits:
                    handle.wait_ge(sem_handles[s], v)
                inst = fn(handle)
                if inc is not None:
                    inst.then_inc(sem_handles[inc], amt)
            if final:
                for (s, v) in final:
                    handle.wait_ge(sem_handles[s], v)

        @block.tensor
        def _(tensor):
            runner(tensor, "te")

        @block.scalar
        def _(scalar):
            runner(scalar, "act")

        @block.vector
        def _(vector):
            runner(vector, "dve")

        @block.sync
        def _(sync):
            runner(sync, "qs", final=[("QS", final_qs)])

        @block.gpsimd
        def _(gpsimd):
            runner(gpsimd, "qp", final=[("QS", final_qs)])

        for cm in sem_cms:
            cm.__exit__(None, None, None)

    for cm in reversed(ctx):
        cm.__exit__(None, None, None)
    return nc, E, names


def kernel(**inputs):
    inp = {k: np.asarray(v, dtype=np.float32) for k, v in inputs.items()}
    x = inp["L0_attn_x"][0]
    bf16 = ml_dtypes.bfloat16

    def C(a, dt=np.float32):
        return np.ascontiguousarray(np.asarray(a, dtype=dt))

    shared = {
        "wqkv": C(inp["L0_attn_w_qkv"]),
        "bq": C(inp["L0_attn_b_qkv"].reshape(48, 128).T),
        "wout": C(inp["L0_attn_w_out"]),
        "bo": C(inp["L0_attn_b_out"].reshape(16, 128).T),
        "wqT": C(inp["L0_ffn_Wq"].T),
        "wkT": C(inp["L0_ffn_Wk"].T),
        "wvT": C(inp["L0_ffn_Wv"].T),
        "woT": C(inp["L0_ffn_Wo"].T),
        "amat": C(inp["L0_ffn_A"]),
        "bmat": C(inp["L0_ffn_B"]),
        "winT": C(inp["L0_ffn_W_in_base"].T),
        "wupT": C(inp["L0_ffn_W_up_base"].T),
        "wdnT": C(inp["L0_ffn_W_down"].T),
        "ln1": C(inp["L0_ffn_ln1_w"].reshape(16, 128).T),
        "ln2": C(inp["L0_ffn_ln2_w"].reshape(16, 128).T),
        "identd": C(np.eye(128), bf16),
        "onescd": C(np.ones((128, 1)), bf16),
        "onesrd": C(np.ones((1, 128))),
        "constv": C(np.tile(np.array([[1e-5, 1e-6, -30.0]], dtype=np.float32), (128, 1))),
    }
    p = np.arange(128)
    kt = np.arange(64)
    shared["kcold"] = C(8.0 * (128.0 * (kt[None, :] % 8) + p[:, None]) + (kt[None, :] // 8))
    toks = [np.arange(1024) * 8 + c for c in range(NCORES)]
    percore = {}
    for c in range(NCORES):
        percore[c] = {
            "xT": C(x[toks[c]].T),
            "qgd": C((8.0 * np.arange(1024) + c)[None, :]),
        }

    def run_part(part, extra_percore):
        nc, _, names = build_nc(part)
        in_maps = []
        for c in range(NCORES):
            m = {}
            for n in names:
                if n in shared:
                    m[n] = shared[n]
                elif n in percore[c]:
                    m[n] = percore[c][n]
                else:
                    m[n] = extra_percore[n][c] if isinstance(extra_percore[n], list) \
                        else extra_percore[n]
            in_maps.append(m)
        return run_bass_kernel_spmd(nc, in_maps, core_ids=list(range(NCORES)))

    r1 = run_part(1, {})
    qaT = [np.asarray(r1.results[c]["qaT"]) for c in range(NCORES)]
    kvAf = np.ascontiguousarray(np.concatenate(
        [np.asarray(r1.results[c]["kvA"]) for c in range(NCORES)], axis=0))
    r2 = run_part(2, {"qaT": qaT, "kvAf": kvAf})
    qbT = [np.asarray(r2.results[c]["qbT"]) for c in range(NCORES)]
    x1o = [np.asarray(r2.results[c]["x1o"]) for c in range(NCORES)]
    kvBf = np.ascontiguousarray(np.concatenate(
        [np.asarray(r2.results[c]["kvB"]) for c in range(NCORES)], axis=0))
    r3 = run_part(3, {"qbT": qbT, "kvBf": kvBf, "x1i": x1o})
    y = np.empty((8192, 2048), np.float32)
    for c in range(NCORES):
        y[toks[c]] = np.asarray(r3.results[c]["yT"], dtype=np.float32).T
    return y.reshape(1, 8192, 2048)


if __name__ == "__main__":
    for part in (1, 2, 3):
        nc, E, names = build_nc(part)
        print("part", part, {e: len(E.ops[e]) for e in ENGS}, "in:", names)
    print("built ok")
